# version 15
# baseline (speedup 1.0000x reference)
import numpy as np
import ml_dtypes
from contextlib import ExitStack

import concourse.bass as bass
import concourse.mybir as mybir
from concourse.bass_utils import run_bass_kernel_spmd

F32 = mybir.dt.float32
BF16 = mybir.dt.bfloat16
I32 = mybir.dt.int32
AF = mybir.ActivationFunctionType
ALU = mybir.AluOpType
bfdt = ml_dtypes.bfloat16

D = 2048
T = 2048
I = 8192
KC = D // 128
NTB = 4
TB = 512
NPAIR = 16


def _consts():
    inv = 1.0 / 10000.0 ** (np.arange(0, 64, 2, dtype=np.float64) / 64.0)
    t = np.arange(T, dtype=np.float64)
    f = np.outer(t, inv)
    COS = np.tile(np.cos(f).T, (4, 1)).astype(np.float32).astype(bfdt)
    sgn = np.where((np.arange(128) // 32) % 2 == 0, -1.0, 1.0)[:, None]
    SINS = (np.tile(np.sin(f).T, (4, 1)) * sgn).astype(np.float32).astype(bfdt)
    m = np.arange(128)
    nb = (m // 64) * 64 + 2 * (m % 32) + ((m % 64) // 32)
    sigma = np.where(m % 64 < 32, m + 32, m - 32)
    PNB = np.zeros((128, 128), dtype=bfdt)
    PNB[nb, m] = 1
    PCMB = np.zeros((128, 128), dtype=bfdt)
    PCMB[nb[sigma], m] = 1
    DUPLO = np.zeros((128, 128), dtype=bfdt)
    DUPLO[m % 64, m] = 1
    DUPHI = np.zeros((128, 128), dtype=bfdt)
    DUPHI[64 + (m % 64), m] = 1
    IDN = np.zeros((128, 128), dtype=bfdt)
    IDN[np.arange(128), np.arange(128)] = 1
    NIDN = np.zeros((128, 128), dtype=bfdt)
    NIDN[np.arange(128), np.arange(128)] = -1.0e5
    MASKS = []
    tk = np.arange(128)[:, None]
    tq = np.arange(512)[None, :]
    for v in range(4):
        MASKS.append((tk > tq - 128 * v).astype(np.float32).astype(bfdt))
    return COS, SINS, PNB, PCMB, DUPLO, DUPHI, IDN, NIDN, MASKS


def build_nc():
    nc = bass.Bass("TRN2", target_bir_lowering=False)
    x_h = nc.dram_tensor("x", [T, D], F32, kind="ExternalInput").ap()
    wq_h = nc.dram_tensor("wq", [D, D], F32, kind="ExternalInput").ap()
    wk_h = nc.dram_tensor("wk", [512, D], F32, kind="ExternalInput").ap()
    wv_h = nc.dram_tensor("wv", [512, D], F32, kind="ExternalInput").ap()
    wo_h = nc.dram_tensor("wo", [D, D], F32, kind="ExternalInput").ap()
    gw_h = nc.dram_tensor("gw", [I, D], I32, kind="ExternalInput").ap()
    uw_h = nc.dram_tensor("uw", [I, D], I32, kind="ExternalInput").ap()
    dw_h = nc.dram_tensor("dw", [D, I], I32, kind="ExternalInput").ap()
    gs_h = nc.dram_tensor("gs", [I], F32, kind="ExternalInput").ap()
    us_h = nc.dram_tensor("us", [I], F32, kind="ExternalInput").ap()
    ds_h = nc.dram_tensor("ds", [D], F32, kind="ExternalInput").ap()
    out_h = nc.dram_tensor("out", [T, D], F32, kind="ExternalOutput").ap()
    scr_wq = nc.dram_tensor("scr_wq", [16, 128, 2048], BF16).ap()
    scr_wk = nc.dram_tensor("scr_wk", [4, 128, 2048], BF16).ap()
    scr_wv = nc.dram_tensor("scr_wv", [128, 16, 512], BF16).ap()
    scr_wo = nc.dram_tensor("scr_wo", [16, 128, 2048], BF16).ap()
    scr_g = nc.dram_tensor("scr_g", [64, 128, 2048], BF16).ap()
    scr_u = nc.dram_tensor("scr_u", [64, 128, 2048], BF16).ap()
    scr_d = nc.dram_tensor("scr_d", [16, 128, 8192], BF16).ap()
    scr_xn = nc.dram_tensor("scr_xn", [T, D], BF16).ap()
    scr_at = nc.dram_tensor("scr_at", [D, T], BF16).ap()
    scr_ffn = nc.dram_tensor("scr_ffn", [D, T], BF16).ap()
    COS_c, SINS_c, PNB_c, PCMB_c, DLO_c, DHI_c, IDN_c, NIDN_c, MASKS_c = _consts()
    cos_d = nc.inline_tensor(COS_c, name="cos_c").ap()
    sin_d = nc.inline_tensor(SINS_c, name="sin_c").ap()
    pnb_d = nc.inline_tensor(PNB_c, name="pnb_c").ap()
    pcmb_d = nc.inline_tensor(PCMB_c, name="pcmb_c").ap()
    dlo_d = nc.inline_tensor(DLO_c, name="dlo_c").ap()
    dhi_d = nc.inline_tensor(DHI_c, name="dhi_c").ap()
    idn_d = nc.inline_tensor(IDN_c, name="idn_c").ap()
    nidn_d = nc.inline_tensor(NIDN_c, name="nidn_c").ap()
    msk_d = [nc.inline_tensor(MASKS_c[v], name=f"msk{v}_c").ap() for v in range(4)]

    es = ExitStack()

    class S:
        def __init__(self, name):
            self.h = es.enter_context(nc.semaphore(name))
            self.v = 0
    sl = S("sl")
    ss = S("ss")
    sg = S("sg")
    spe = S("spe")
    sac = S("sac")
    sdv = S("sdv")
    spl = S("spl")
    sout = S("sout")
    sxn = S("sxn")

    QSEM = {"sync": sl, "scalar": ss, "gpsimd": sg}
    ESEM = {"tensor": spe, "scalar": sac, "vector": sdv, "gpsimd": spl}

    ENGS = ["sync", "scalar", "vector", "tensor", "gpsimd"]
    queues = {e: [] for e in ENGS}

    def flush_block():
        with nc.Block() as blk:
            for name in ENGS:
                fns = queues[name]
                if fns:
                    getattr(blk, name)(lambda e, fns=fns: [f(e) for f in fns])
        for name in ENGS:
            queues[name] = []

    def emit(eng, f):
        queues[eng].append(f)

    def wait(eng, sem, v=None):
        vv = sem.v if v is None else v
        if vv <= 0:
            return
        emit(eng, lambda e, s=sem.h, v=vv: e.wait_ge(s, v))

    def dma(eng, out, in_, **kw):
        sem = QSEM[eng]
        sem.v += 16
        emit(eng, lambda e, o=out, i=in_, k=kw, s=sem.h: e.dma_start(out=o, in_=i, **k).then_inc(s, 16))
        return sem.v

    def inc(eng, call):
        sem = ESEM[eng]
        sem.v += 1
        emit(eng, lambda e, c=call, s=sem.h: c(e).then_inc(s, 1))
        return sem.v

    def op(eng, call):
        emit(eng, call)

    def drain_dmas():
        wait("sync", sl, sl.v)
        wait("scalar", ss, ss.v)
        wait("scalar", sxn, sxn.v)
        wait("gpsimd", sg, sg.v)

    def sb_in(stack, name, shape, dt):
        return stack.enter_context(nc.sbuf_tensor(name, shape, dt))

    COS = sb_in(es, "COS", [128, T], BF16)
    SINS = sb_in(es, "SINS", [128, T], BF16)
    PNB = sb_in(es, "PNB", [128, 128], BF16)
    PCMB = sb_in(es, "PCMB", [128, 128], BF16)
    DLO = sb_in(es, "DLO", [128, 128], BF16)
    DHI = sb_in(es, "DHI", [128, 128], BF16)
    IDN = sb_in(es, "IDN", [128, 128], BF16)
    MSK = [sb_in(es, f"MSK{v}", [128, 512], BF16) for v in range(4)]
    NIDN = sb_in(es, "NIDN", [128, 128], BF16)
    gs_sb = sb_in(es, "gs_sb", [128, 64], F32)
    us_sb = sb_in(es, "us_sb", [128, 64], F32)
    ds_sb = sb_in(es, "ds_sb", [128, 16], F32)
    ones64 = sb_in(es, "ones64", [1, 64], BF16)
    epsv = sb_in(es, "epsv", [128, 1], F32)
    n128v = sb_in(es, "n128v", [128, 1], F32)

    op("gpsimd", lambda e: e.memset(ones64[:], 1.0))
    op("gpsimd", lambda e: e.memset(epsv[:], 1e-5))
    op("gpsimd", lambda e: e.memset(n128v[:], -128.0))
    dma("sync", COS[:], cos_d)
    dma("sync", SINS[:], sin_d)
    dma("sync", PNB[:], pnb_d)
    dma("sync", PCMB[:], pcmb_d)
    dma("sync", DLO[:], dlo_d)
    dma("sync", DHI[:], dhi_d)
    dma("sync", IDN[:], idn_d)
    dma("sync", NIDN[:], nidn_d)
    for v in range(4):
        dma("sync", MSK[v][:], msk_d[v])
    for c in range(64):
        dma("sync", gs_sb[:, c:c + 1], gs_h[c * 128:(c + 1) * 128][:, None])
        dma("sync", us_sb[:, c:c + 1], us_h[c * 128:(c + 1) * 128][:, None])
    for c in range(16):
        dma("sync", ds_sb[:, c:c + 1], ds_h[c * 128:(c + 1) * 128][:, None])
    wait("tensor", sl, sl.v)

    es1 = ExitStack()
    q_sb = sb_in(es1, "q_sb", [128, 16, T], BF16)
    kd_sb = sb_in(es1, "kd_sb", [128, 8, T], BF16)
    v_sb = sb_in(es1, "v_sb", [128, 16, 520], BF16)

    esp = ExitStack()
    stg32 = [sb_in(esp, f"stg32_{i}", [128, 2048], I32) for i in range(2)]
    stg16 = [sb_in(esp, f"stg16_{i}", [128, 2048], BF16) for i in range(2)]
    stgT = [sb_in(esp, f"stgT{i}", [128, 2048], BF16) for i in range(2)]

    pesp = ExitStack()
    pst = [pesp.enter_context(nc.psum_tensor(f"pst{i}", [128, 512], BF16)) for i in range(4)]

    prep = {
        "cast_done": [0, 0],
        "tp_done": [0, 0],
        "evac_done": [("vector", 0)] * 4,
        "stgT_free": [0, 0],
        "n": 0,
    }

    def make_tasks(src_h, n_chunks, W, dst_fn, sub128, src_dt, done_arr, tag):
        ts = []
        ns = W // 2048
        for c in range(n_chunks):
            for s in range(ns):
                ts.append(dict(src=src_h[c * 128:(c + 1) * 128, s * 2048:(s + 1) * 2048],
                               dst=dst_fn(c, s), sub=sub128, dt=src_dt,
                               done=done_arr, c=c, last=(s == ns - 1), tag=tag))
        return ts

    qdone = [0] * 16
    kdone = [0] * 4
    vdone = [0]
    odone = [0] * 16
    gdone = [0] * 64
    udone = [0] * 64
    ddone = [0] * 16

    tasks_attn = (
        make_tasks(wq_h, 16, D, lambda c, s: scr_wq[c][:, :], False, F32, qdone, "q")
        + make_tasks(wk_h, 4, D, lambda c, s: scr_wk[c][:, :], False, F32, kdone, "k")
        + make_tasks(wv_h, 4, D, lambda c, s: scr_wv[:, :, c * 128:(c + 1) * 128], False, F32, vdone, "v")
    )
    tasks_rest = make_tasks(wo_h, 16, D, lambda c, s: scr_wo[c][:, :], False, F32, odone, "o")
    tg = make_tasks(gw_h, 64, D, lambda c, s: scr_g[c][:, :], True, I32, gdone, "g")
    tu = make_tasks(uw_h, 64, D, lambda c, s: scr_u[c][:, :], True, I32, udone, "u")
    for a, b in zip(tg, tu):
        tasks_rest.append(a)
        tasks_rest.append(b)
    tasks_rest += make_tasks(dw_h, 16, I, lambda c, s: scr_d[c][:, s * 2048:(s + 1) * 2048], True, I32, ddone, "d")

    pending = tasks_attn + tasks_rest

    EVAC = {"vector": sdv, "scalar": sac}

    def pump(n):
        for _ in range(n):
            if not pending:
                return
            t = pending.pop(0)
            k = prep["n"]
            prep["n"] += 1
            i = k % 2
            j = k % 2
            ldq = "scalar" if (k % 2 == 0) else "gpsimd"
            ldsem = QSEM[ldq]
            wait(ldq, sac, prep["cast_done"][i])
            ld = dma(ldq, stg32[i][:, :] if t["dt"] == I32 else
                     stg32[i].ap().bitcast(F32)[:, 0:2048], t["src"])
            wait("scalar", ldsem, ld)
            wait("scalar", spe, prep["tp_done"][i])
            if t["sub"]:
                cd = inc("scalar", lambda e, i=i: e.activation(
                    out=stg16[i][:, :], in_=stg32[i][:, 0:2048], func=AF.Identity, bias=n128v[:]))
            else:
                cd = inc("scalar", lambda e, i=i: e.activation(
                    out=stg16[i][:, :], in_=stg32[i].ap().bitcast(F32)[:, 0:2048], func=AF.Copy))
            prep["cast_done"][i] = cd
            wait("tensor", sac, cd)
            ev_vals = {"vector": 0, "scalar": 0}
            first_evac = True
            for tt in range(4):
                (pe_eng, pe_val) = prep["evac_done"][tt]
                wait("tensor", EVAC[pe_eng], pe_val)
                tpl = 0
                for q in range(4):
                    kc = 4 * tt + q
                    tpl = inc("tensor", lambda e, tt=tt, q=q, kc=kc, i=i: e.transpose(
                        out=pst[tt][:, q * 128:(q + 1) * 128],
                        in_=stg16[i][:, kc * 128:(kc + 1) * 128], identity=IDN[:]))
                if tt == 3:
                    prep["tp_done"][i] = tpl
                eng = "vector" if (k * 4 + tt) % 2 == 0 else "scalar"
                if first_evac:
                    wait(eng, sl, prep["stgT_free"][j])
                wait(eng, spe, tpl)
                if eng == "vector":
                    ev = inc(eng, lambda e, tt=tt, j=j: e.tensor_copy(
                        out=stgT[j][:, tt * 512:(tt + 1) * 512], in_=pst[tt][:, 0:512]))
                else:
                    ev = inc(eng, lambda e, tt=tt, j=j: e.activation(
                        out=stgT[j][:, tt * 512:(tt + 1) * 512], in_=pst[tt][:, 0:512], func=AF.Copy))
                prep["evac_done"][tt] = (eng, ev)
                ev_vals[eng] = ev
                first_evac = False
            wait("sync", sdv, ev_vals["vector"])
            wait("sync", sac, ev_vals["scalar"])
            dst = t["dst"]
            if t["tag"] == "v":
                st = dma("sync", dst, stgT[j][:, 0:2048].rearrange("p (a b) -> p a b", a=16))
            else:
                st = dma("sync", dst, stgT[j][:, 0:2048])
            prep["stgT_free"][j] = st
            if t["tag"] == "v":
                vdone[0] = st
            elif t["last"]:
                t["done"][t["c"]] = st

    pump(len(pending))
    drain_dmas()
    flush_block()
    pesp.close()
    esp.close()

    es1b = ExitStack()
    xt_s = sb_in(es1b, "xt0", [128, D], F32)
    xt = [xt_s, xt_s]
    xn_tm = sb_in(es1b, "xn_tm", [128, D], BF16)
    xn_fm = sb_in(es1b, "xn_fm", [128, KC, TB], BF16)
    sumv = sb_in(es1b, "sumv", [128, 1], F32)
    sqv = sb_in(es1b, "sqv", [128, 1], F32)
    mv = sb_in(es1b, "mv", [128, 1], F32)
    ex2v = sb_in(es1b, "ex2v", [128, 1], F32)
    varv = sb_in(es1b, "varv", [128, 1], F32)
    rstdv = sb_in(es1b, "rstdv", [128, 1], F32)
    nmrv = sb_in(es1b, "nmrv", [128, 1], F32)
    wbuf = [sb_in(es1b, f"wbuf{i}", [128, 2048], BF16) for i in range(2)]
    wvb = sb_in(es1b, "wvb", [128, 16, 512], BF16)
    qtmp = [sb_in(es1b, f"qtmp{i}", [128, 512], BF16) for i in range(2)]
    t1b = sb_in(es1b, "t1b", [128, 512], BF16)
    t2b = sb_in(es1b, "t2b", [128, 512], BF16)
    krope = sb_in(es1b, "krope", [128, 4, 512], BF16)

    pes = ExitStack()
    pp = [pes.enter_context(nc.psum_tensor(f"pp{i}", [128, 512], F32)) for i in range(2)]
    pa_ps = pes.enter_context(nc.psum_tensor("pa_ps", [128, 512], F32))
    pb_ps = pes.enter_context(nc.psum_tensor("pb_ps", [128, 512], F32))

    for tt in range(16):
        for j in range(8):
            op("vector", lambda e, tt=tt, j=j: e.memset(v_sb[:, tt, j * 65 + 64:j * 65 + 65], 1.0))

    xt_rd = [0]
    xntm_rd = [0]
    sqv_rd = [0]
    qtmp_st = [(0, 0), (0, 0)]
    pp_rd = [0, 0]
    pa_rd = [0]
    pb_rd = [0]
    rope_flip = [0]

    def ln_tile(b, tau):
        t0 = (4 * b + tau) * 128
        xe = xt[tau % 2]
        wait("sync", sac, xt_rd[0])
        lv = dma("sync", xe[:], x_h[t0:t0 + 128, :])
        wait("vector", sl, lv)
        sdv_red = inc("vector", lambda e, xe=xe: e.reduce_sum(sumv[:], xe[:], axis=mybir.AxisListType.X))
        wait("scalar", sl, lv)
        wait("scalar", sdv, sqv_rd[0])
        wait("scalar", sxn, xntm_rd[0])
        sac_sq = inc("scalar", lambda e, xe=xe: e.activation(
            out=xn_tm[:], in_=xe[:], func=AF.Square, accum_out=sqv[:]))
        wait("vector", sdv, sdv_red)
        sdv_mv = inc("vector", lambda e: e.tensor_scalar_mul(out=mv[:], in0=sumv[:], scalar1=1.0 / D))
        wait("vector", sac, sac_sq)
        sqv_rd[0] = inc("vector", lambda e: e.tensor_scalar_mul(out=ex2v[:], in0=sqv[:], scalar1=1.0 / D))
        wait("vector", sdv, sdv_mv)
        sdv_vm = inc("vector", lambda e: e.tensor_tensor(out=varv[:], in0=mv[:], in1=mv[:], op=ALU.mult))
        wait("vector", sdv, sdv_vm)
        sdv_var = inc("vector", lambda e: e.tensor_tensor(out=varv[:], in0=ex2v[:], in1=varv[:], op=ALU.subtract))
        wait("scalar", sdv, sdv_var)
        sac_ln1 = inc("scalar", lambda e: e.activation(out=rstdv[:], in_=varv[:], func=AF.Ln, bias=epsv[:]))
        wait("scalar", sac, sac_ln1)
        sac_r = inc("scalar", lambda e: e.activation(out=rstdv[:], in_=rstdv[:], func=AF.Exp, scale=-0.5))
        wait("vector", sac, sac_r)
        sdv_nmr = inc("vector", lambda e: e.tensor_tensor(out=nmrv[:], in0=mv[:], in1=rstdv[:], op=ALU.mult))
        wait("vector", sdv, sdv_nmr)
        sdv_nmr2 = inc("vector", lambda e: e.tensor_scalar_mul(out=nmrv[:], in0=nmrv[:], scalar1=-1.0))
        wait("scalar", sdv, sdv_nmr2)
        sac_xn = inc("scalar", lambda e, xe=xe: e.activation(
            out=xn_tm[:], in_=xe[:], func=AF.Identity, scale=rstdv[:], bias=nmrv[:]))
        xt_rd[0] = sac_xn
        wait("scalar", sac, sac_xn)
        sxn.v += 16
        emit("scalar", lambda e, t0=t0, s=sxn.h: e.dma_start(
            out=scr_xn[t0:t0 + 128, :], in_=xn_tm[:]).then_inc(s, 16))
        xntm_rd[0] = sxn.v

    def rope_chunk(b, ps_src, ps_idx, dst_ap):
        fl = rope_flip[0]
        rope_flip[0] ^= 1
        qt = qtmp[fl]
        (g_spe, g_sdv) = qtmp_st[fl]
        wait("scalar", spe, g_spe)
        wait("scalar", sdv, g_sdv)
        sac_ev = inc("scalar", lambda e, qt=qt, p=ps_src: e.activation(out=qt[:], in_=p[:], func=AF.Copy))
        if ps_idx is not None:
            pp_rd[ps_idx] = sac_ev
        wait("tensor", sac, sac_ev)
        wait("tensor", sdv, pa_rd[0])
        spe_a = inc("tensor", lambda e, qt=qt: e.matmul(out=pa_ps[:], lhsT=PNB[:], rhs=qt[:],
                                                        start=True, stop=True))
        wait("tensor", sdv, pb_rd[0])
        spe_b = inc("tensor", lambda e, qt=qt: e.matmul(out=pb_ps[:], lhsT=PCMB[:], rhs=qt[:],
                                                        start=True, stop=True))
        bsl = slice(b * TB, (b + 1) * TB)
        wait("vector", spe, spe_a)
        sdv_t1 = inc("vector", lambda e, bsl=bsl: e.tensor_tensor(
            out=t1b[:], in0=pa_ps[:], in1=COS[:, bsl], op=ALU.mult))
        pa_rd[0] = sdv_t1
        wait("vector", spe, spe_b)
        sdv_t2 = inc("vector", lambda e, bsl=bsl: e.tensor_tensor(
            out=t2b[:], in0=pb_ps[:], in1=SINS[:, bsl], op=ALU.mult))
        pb_rd[0] = sdv_t2
        wait("vector", sdv, sdv_t2)
        sdv_o = inc("vector", lambda e, d=dst_ap: e.tensor_tensor(out=d, in0=t1b[:], in1=t2b[:], op=ALU.add))
        qtmp_st[fl] = (spe_b, sdv_t1)
        return sac_ev, sdv_o

    wbuf_rd = [0, 0]
    wvb_rd = [0]
    v_ev = [0]

    for b in range(NTB):
        bsl = slice(b * TB, (b + 1) * TB)
        for tau in range(4):
            ln_tile(b, tau)
        wait("sync", sxn, sxn.v)
        for c in range(KC):
            dma("sync", xn_fm[:, c, :], scr_xn[bsl, c * 128:(c + 1) * 128], transpose=True)
        xn_ready = sl.v
        for mu in range(16):
            wb = wbuf[mu % 2]
            wait("sync", spe, wbuf_rd[mu % 2])
            wait("sync", sl, qdone[mu])
            dma("sync", wb[:, :], scr_wq[mu])
            ld_done = sl.v
            pe = pp[mu % 2]
            wait("tensor", sl, ld_done)
            wait("tensor", sl, xn_ready)
            wait("tensor", sac, pp_rd[mu % 2])
            last = 0
            for k in range(KC):
                last = inc("tensor", lambda e, k=k, pe=pe, wb=wb: e.matmul(
                    out=pe[:], lhsT=wb[:, k * 128:(k + 1) * 128], rhs=xn_fm[:, k, :],
                    start=(k == 0), stop=(k == KC - 1)))
            wbuf_rd[mu % 2] = last
            wait("scalar", spe, last)
            rope_chunk(b, pe, mu % 2, q_sb[:, mu, bsl])
        for mu in range(4):
            wbk = wbuf[mu % 2]
            wait("sync", spe, wbuf_rd[mu % 2])
            wait("sync", sl, kdone[mu])
            dma("sync", wbk[:, :], scr_wk[mu])
            ld_done = sl.v
            pe = pp[mu % 2]
            wait("tensor", sl, ld_done)
            wait("tensor", sl, xn_ready)
            wait("tensor", sac, pp_rd[mu % 2])
            last = 0
            for k in range(KC):
                last = inc("tensor", lambda e, k=k, pe=pe, wbk=wbk: e.matmul(
                    out=pe[:], lhsT=wbk[:, k * 128:(k + 1) * 128], rhs=xn_fm[:, k, :],
                    start=(k == 0), stop=(k == KC - 1)))
            wbuf_rd[mu % 2] = last
            wait("scalar", spe, last)
            _, sdv_o = rope_chunk(b, pe, mu % 2, krope[:, mu, :])
            wait("tensor", sdv, sdv_o)
            for half, PM in ((0, DLO), (1, DHI)):
                wait("tensor", sdv, pa_rd[0])
                spe_d = inc("tensor", lambda e, PM=PM, mu=mu: e.matmul(
                    out=pa_ps[:], lhsT=PM[:], rhs=krope[:, mu, :], start=True, stop=True))
                wait("scalar", spe, spe_d)
                sac_e = inc("scalar", lambda e, mu=mu, half=half, bsl=bsl: e.activation(
                    out=kd_sb[:, 2 * mu + half, bsl], in_=pa_ps[:], func=AF.Copy))
                pa_rd[0] = 0
                wait("tensor", sac, sac_e)
        wait("sync", spe, wvb_rd[0])
        wait("sync", sl, vdone[0])
        dma("sync", wvb[:, :, :], scr_wv)
        ld_done = sl.v
        for tau in range(4):
            pe0 = pp[tau % 2]
            wait("tensor", sl, ld_done)
            wait("tensor", sl, xn_ready)
            wait("tensor", sac, pp_rd[tau % 2])
            wait("tensor", sdv, v_ev[0])
            last = 0
            for k in range(KC):
                last = inc("tensor", lambda e, k=k, pe0=pe0, tau=tau: e.matmul(
                    out=pe0[:], lhsT=xn_fm[:, k, tau * 128:(tau + 1) * 128],
                    rhs=wvb[:, k, :], start=(k == 0), stop=(k == KC - 1)))
            wvb_rd[0] = last
            wait("vector", spe, last)
            vout = v_sb[:, 4 * b + tau, 0:520].rearrange("p (h x) -> p h x", h=8)[:, :, 0:64]
            vin = pe0[:, 0:512].rearrange("p (h x) -> p h x", h=8)
            sdv_v = inc("vector", lambda e, vout=vout, vin=vin: e.tensor_copy(out=vout, in_=vin))
            v_ev[0] = sdv_v
            wait("tensor", sdv, sdv_v)
            pp_rd[tau % 2] = 0
        if b == NTB - 1:
            drain_dmas()
        flush_block()

    es1b.close()
    pes.close()

    es2 = ExitStack()
    wob = [sb_in(es2, f"wob{i}", [128, 2048], BF16) for i in range(2)]
    pbuf = [sb_in(es2, f"pbuf{i}", [128, 1024], BF16) for i in range(2)]
    afm = sb_in(es2, "afm", [128, 16, 512], BF16)
    lnv2 = sb_in(es2, "lnv2", [1, 512], F32)
    rbf = sb_in(es2, "rbf", [1, 512], BF16)
    rb64 = sb_in(es2, "rb64", [64, 512], BF16)
    tmp64 = sb_in(es2, "tmp64", [64, 512], BF16)
    woev = [sb_in(es2, f"woev{i}", [128, 512], BF16) for i in range(2)]

    pes2 = ExitStack()
    psc = [pes2.enter_context(nc.psum_tensor(f"psc{i}", [128, 1024], F32)) for i in range(2)]
    pav = [pes2.enter_context(nc.psum_tensor(f"pav{i}", [128, 512], F32)) for i in range(4)]

    norm_state = {"rbf_rd": 0, "rb64_rd": 0}

    for B in range(NTB):
        Bsl = slice(B * TB, (B + 1) * TB)
        nck = 4 * B + 4
        for s in range(2):
            wait("sync", sl, odone[s])
            dma("sync", wob[s][:, :], scr_wo[s])
        wo_ld01 = sl.v

        exp_read = {}
        av_tt = {}
        av_last = {}
        norm_q = []

        def do_norm(e_pair, pidx):
            for hh in range(2):
                pa = pav[pidx + hh]
                sac_ln = inc("scalar", lambda e, pa=pa: e.activation(
                    out=lnv2[:], in_=pa[64:65, :], func=AF.Ln))
                wait("scalar", spe, norm_state["rbf_rd"])
                wait("scalar", sac, sac_ln)
                sac_r = inc("scalar", lambda e: e.activation(
                    out=rbf[:], in_=lnv2[:], func=AF.Exp, scale=-1.0))
                wait("tensor", sac, sac_r)
                spe_b = inc("tensor", lambda e, pa=pa: e.matmul(
                    out=pa[64:128, :], lhsT=ones64[:], rhs=rbf[:], start=True, stop=True))
                norm_state["rbf_rd"] = spe_b
                wait("scalar", spe, spe_b)
                wait("scalar", sdv, norm_state["rb64_rd"])
                sac_rb = inc("scalar", lambda e, pa=pa: e.activation(
                    out=rb64[:], in_=pa[64:128, :], func=AF.Copy))
                wait("vector", sac, sac_rb)
                if hh == 0:
                    sdv_tt = inc("vector", lambda e, pa=pa, ep=e_pair: e.tensor_tensor(
                        out=afm[0:64, ep, :], in0=pa[0:64, :], in1=rb64[:], op=ALU.mult))
                else:
                    sdv_t64 = inc("vector", lambda e, pa=pa: e.tensor_tensor(
                        out=tmp64[:], in0=pa[0:64, :], in1=rb64[:], op=ALU.mult))
                    wait("vector", sdv, sdv_t64)
                    sdv_tt = inc("vector", lambda e, ep=e_pair: e.tensor_scalar_mul(
                        out=afm[64:128, ep, :], in0=tmp64[:], scalar1=1.0))
                norm_state["rb64_rd"] = sdv_tt
                av_tt[pidx + hh] = sdv_tt

        for e_pair in range(NPAIR):
            j = e_pair // 2
            pidx = (e_pair % 2) * 2
            for hh in range(2):
                if (pidx + hh) in av_tt:
                    wait("tensor", sdv, av_tt[pidx + hh])
            av_q = []
            for c in range(nck):
                ps = psc[c % 2]
                if (c % 2) in exp_read:
                    wait("tensor", sac, exp_read[c % 2])
                inc("tensor", lambda e, ps=ps, j=j, c=c, ep=e_pair, Bsl=Bsl: e.matmul(
                    out=ps[:, 0:512], lhsT=kd_sb[0:64, j, c * 128:(c + 1) * 128],
                    rhs=q_sb[0:64, ep, Bsl], start=True, stop=(c < 4 * B)))
                spe_s = inc("tensor", lambda e, ps=ps, j=j, c=c, ep=e_pair, Bsl=Bsl: e.matmul(
                    out=ps[:, 512:1024], lhsT=kd_sb[64:128, j, c * 128:(c + 1) * 128],
                    rhs=q_sb[64:128, ep, Bsl], start=True, stop=(c < 4 * B)))
                if c >= 4 * B:
                    vmk = MSK[c - 4 * B]
                    inc("tensor", lambda e, ps=ps, vmk=vmk: e.matmul(
                        out=ps[:, 0:512], lhsT=NIDN[:], rhs=vmk[:], start=False, stop=True))
                    spe_s = inc("tensor", lambda e, ps=ps, vmk=vmk: e.matmul(
                        out=ps[:, 512:1024], lhsT=NIDN[:], rhs=vmk[:], start=False, stop=True))
                if av_q:
                    av_q.pop(0)()
                wait("scalar", spe, spe_s)
                if (c % 2) in av_last:
                    wait("scalar", spe, av_last[c % 2])
                sac_e = inc("scalar", lambda e, ps=ps, c=c: e.activation(
                    out=pbuf[c % 2][:], in_=ps[:], func=AF.Exp, scale=0.125))
                exp_read[c % 2] = sac_e

                def do_av(c=c, j=j, pidx=pidx, nck=nck, sac_e=sac_e):
                    wait("tensor", sac, sac_e)
                    inc("tensor", lambda e, c=c, j=j, pidx=pidx, nck=nck: e.matmul(
                        out=pav[pidx][0:65, :], lhsT=v_sb[:, c, j * 65:(j + 1) * 65],
                        rhs=pbuf[c % 2][:, 0:512], start=(c == 0), stop=(c == nck - 1)))
                    spe_av = inc("tensor", lambda e, c=c, j=j, pidx=pidx, nck=nck: e.matmul(
                        out=pav[pidx + 1][0:65, :], lhsT=v_sb[:, c, j * 65:(j + 1) * 65],
                        rhs=pbuf[c % 2][:, 512:1024], start=(c == 0), stop=(c == nck - 1)))
                    av_last[c % 2] = spe_av
                av_q.append(do_av)
            while av_q:
                av_q.pop(0)()
            if norm_q:
                norm_q.pop(0)()
            norm_q.append(lambda ep=e_pair, pidx=pidx: do_norm(ep, pidx))
        while norm_q:
            norm_q.pop(0)()
        attn_done = sdv.v
        woev_rd = [0, 0]
        wob_rd = [0, 0]
        for mu in range(16):
            wot = wob[mu % 2]
            if mu >= 2:
                wait("sync", spe, wob_rd[mu % 2])
                wait("sync", sl, odone[mu])
                dma("sync", wot[:, :], scr_wo[mu])
                ld_done = sl.v
            else:
                ld_done = wo_ld01
            pw = psc[mu % 2]
            wait("tensor", sl, ld_done)
            wait("tensor", sdv, attn_done)
            if (mu % 2) in exp_read:
                wait("tensor", sac, exp_read.pop(mu % 2))
            last = 0
            for dv in range(16):
                last = inc("tensor", lambda e, dv=dv, pw=pw, wot=wot: e.matmul(
                    out=pw[:, 0:512], lhsT=wot[:, dv * 128:(dv + 1) * 128], rhs=afm[:, dv, :],
                    start=(dv == 0), stop=(dv == 15)))
            wob_rd[mu % 2] = last
            wait("scalar", spe, last)
            wait("scalar", ss, woev_rd[mu % 2])
            sac_ev = inc("scalar", lambda e, pw=pw, mu=mu: e.activation(
                out=woev[mu % 2][:], in_=pw[:, 0:512], func=AF.Copy))
            exp_read[mu % 2] = sac_ev
            wait("scalar", sac, sac_ev)
            dma("scalar", scr_at[mu * 128:(mu + 1) * 128, Bsl], woev[mu % 2][:])
            woev_rd[mu % 2] = ss.v
        if B == NTB - 1:
            drain_dmas()
        flush_block()

    pes2.close()
    es2.close()
    es1.close()

    es3 = ExitStack()
    h_sb = sb_in(es3, "h_sb", [128, 64, 512], BF16)
    xnf = sb_in(es3, "xnf", [128, KC, 512], BF16)
    gwide = [sb_in(es3, f"gwide{i}", [128, 2, 2048], BF16) for i in range(2)]
    uwide = [sb_in(es3, f"uwide{i}", [128, 2, 2048], BF16) for i in range(2)]
    dbw = [sb_in(es3, f"dbw{i}", [128, 4096], BF16) for i in range(2)]
    sgt = [sb_in(es3, f"sgt{i}", [128, 512], BF16) for i in range(2)]
    sut = [sb_in(es3, f"sut{i}", [128, 512], BF16) for i in range(2)]
    fdev = [sb_in(es3, f"fdev{i}", [128, 512], BF16) for i in range(2)]
    xt3 = [sb_in(es3, f"xt3{i}", [128, D], F32) for i in range(2)]
    at3 = [sb_in(es3, f"at3{i}", [128, D], BF16) for i in range(2)]
    ft3 = [sb_in(es3, f"ft3{i}", [128, D], BF16) for i in range(2)]
    out3 = [sb_in(es3, f"out3{i}", [128, D], F32) for i in range(2)]

    pes3 = ExitStack()
    pg = [pes3.enter_context(nc.psum_tensor(f"pg{i}", [128, 512], F32)) for i in range(4)]
    pu = [pes3.enter_context(nc.psum_tensor(f"pu{i}", [128, 512], F32)) for i in range(4)]

    gu_mm = [0, 0]
    sgt_rd = [0, 0]
    gu_ev_sac = [0, 0]
    gu_ev_sdv = [0, 0]
    dn_mm = {}
    dn_ev = {}
    out3_rd = [0, 0]
    xt3_rd = [0, 0]
    at3_rd = [0, 0]
    fdev_rd = [0, 0]

    for b in range(NTB):
        bsl = slice(b * TB, (b + 1) * TB)
        for c in range(KC):
            dma("sync", xnf[:, c, :], scr_xn[bsl, c * 128:(c + 1) * 128], transpose=True)
        xn_ld = sl.v
        for sp in range(32):
            set_ = sp % 2
            gw_t, uw_t = gwide[set_], uwide[set_]
            wait("sync", spe, gu_mm[set_])
            if b == 0:
                wait("sync", sl, max(gdone[2 * sp + 1], udone[2 * sp + 1]))
            dma("sync", gw_t[:, :, :], scr_g[2 * sp:2 * sp + 2].rearrange("c p w -> p c w"))
            dma("sync", uw_t[:, :, :], scr_u[2 * sp:2 * sp + 2].rearrange("c p w -> p c w"))
            ld_done = sl.v
            pga, pgb_ = pg[2 * set_], pg[2 * set_ + 1]
            pua, pub = pu[2 * set_], pu[2 * set_ + 1]
            wait("tensor", sl, ld_done)
            wait("tensor", sl, xn_ld)
            wait("tensor", sac, gu_ev_sac[set_])
            wait("tensor", sdv, gu_ev_sdv[set_])
            if sp < 2 and dn_ev:
                wait("tensor", sac, max(dn_ev.values()))
            last = 0
            for k in range(KC):
                inc("tensor", lambda e, k=k, gw_t=gw_t, pga=pga: e.matmul(
                    out=pga[:], lhsT=gw_t[:, 0, k * 128:(k + 1) * 128], rhs=xnf[:, k, :],
                    start=(k == 0), stop=(k == KC - 1)))
                inc("tensor", lambda e, k=k, gw_t=gw_t, pgb_=pgb_: e.matmul(
                    out=pgb_[:], lhsT=gw_t[:, 1, k * 128:(k + 1) * 128], rhs=xnf[:, k, :],
                    start=(k == 0), stop=(k == KC - 1)))
                inc("tensor", lambda e, k=k, uw_t=uw_t, pua=pua: e.matmul(
                    out=pua[:], lhsT=uw_t[:, 0, k * 128:(k + 1) * 128], rhs=xnf[:, k, :],
                    start=(k == 0), stop=(k == KC - 1)))
                last = inc("tensor", lambda e, k=k, uw_t=uw_t, pub=pub: e.matmul(
                    out=pub[:], lhsT=uw_t[:, 1, k * 128:(k + 1) * 128], rhs=xnf[:, k, :],
                    start=(k == 0), stop=(k == KC - 1)))
            gu_mm[set_] = last
            wait("scalar", spe, last)
            wait("vector", spe, last)
            e_sac = e_sdv = 0
            for d in range(2):
                ci = sp * 2 + d
                pgx = pga if d == 0 else pgb_
                pux = pua if d == 0 else pub
                wait("scalar", sdv, sgt_rd[d])
                e_sac = inc("scalar", lambda e, pgx=pgx, ci=ci, d=d: e.activation(
                    out=sgt[d][:], in_=pgx[:], func=AF.Silu, scale=gs_sb[:, ci:ci + 1]))
                e_sdv0 = inc("vector", lambda e, pux=pux, ci=ci, d=d: e.tensor_single_scalar(
                    out=sut[d][:], in_=pux[:], scalar=us_sb[:, ci:ci + 1], op=ALU.mult))
                wait("vector", sac, e_sac)
                wait("vector", sdv, e_sdv0)
                e_sdv = inc("vector", lambda e, ci=ci, d=d: e.tensor_tensor(
                    out=h_sb[:, ci, :], in0=sgt[d][:], in1=sut[d][:], op=ALU.mult))
                sgt_rd[d] = e_sdv
            gu_ev_sac[set_] = e_sac
            gu_ev_sdv[set_] = e_sdv
        h_done = sdv.v
        gu_done_sac = sac.v
        pbank = pg + pu
        for mc in range(16):
            bi = mc % 8
            pb_ = pbank[bi]
            last = 0
            for hf in range(2):
                gbuf = dbw[hf]
                wait("sync", spe, dn_mm.get(hf, 0))
                if b == 0 and hf == 0:
                    wait("sync", sl, ddone[mc])
                dma("sync", gbuf[:, :], scr_d[mc][:, hf * 4096:(hf + 1) * 4096])
                ld_done = sl.v
                wait("tensor", sl, ld_done)
                if hf == 0:
                    if mc < 8:
                        wait("tensor", sdv, h_done)
                        wait("tensor", sac, gu_done_sac)
                    if bi in dn_ev:
                        wait("tensor", sac, dn_ev[bi])
                for q in range(32):
                    ic = hf * 32 + q
                    last = inc("tensor", lambda e, pb_=pb_, gbuf=gbuf, q=q, ic=ic: e.matmul(
                        out=pb_[:], lhsT=gbuf[:, q * 128:(q + 1) * 128],
                        rhs=h_sb[:, ic, :], start=(ic == 0), stop=(ic == 63)))
                dn_mm[hf] = last
            wait("scalar", spe, last)
            wait("scalar", ss, fdev_rd[mc % 2])
            sac_d = inc("scalar", lambda e, pb_=pb_, mc=mc: e.activation(
                out=fdev[mc % 2][:], in_=pb_[:], func=AF.Identity, scale=ds_sb[:, mc:mc + 1]))
            dn_ev[bi] = sac_d
            wait("scalar", sac, sac_d)
            dma("scalar", scr_ffn[mc * 128:(mc + 1) * 128, bsl], fdev[mc % 2][:])
            fdev_rd[mc % 2] = ss.v
        fst_done = ss.v
        wait("sync", ss, fst_done)
        for tau in range(4):
            t0 = (4 * b + tau) * 128
            wait("sync", sdv, xt3_rd[tau % 2])
            lx = dma("sync", xt3[tau % 2][:], x_h[t0:t0 + 128, :])
            wait("sync", sdv, at3_rd[tau % 2])
            la = dma("sync", at3[tau % 2][:], scr_at[:, t0:t0 + 128], transpose=True)
            lf = dma("sync", ft3[tau % 2][:], scr_ffn[:, t0:t0 + 128], transpose=True)
            wait("vector", sl, lf)
            wait("vector", sout, out3_rd[tau % 2])
            sdv_a = inc("vector", lambda e, tau=tau: e.tensor_tensor(
                out=out3[tau % 2][:], in0=xt3[tau % 2][:], in1=at3[tau % 2][:], op=ALU.add))
            xt3_rd[tau % 2] = sdv_a
            at3_rd[tau % 2] = sdv_a
            wait("vector", sdv, sdv_a)
            sdv_o = inc("vector", lambda e, tau=tau: e.tensor_tensor(
                out=out3[tau % 2][:], in0=out3[tau % 2][:], in1=ft3[tau % 2][:], op=ALU.add))
            wait("gpsimd", sdv, sdv_o)
            sout.v += 16
            emit("gpsimd", lambda e, t0=t0, tau=tau, s=sout.h: e.dma_start(
                out=out_h[t0:t0 + 128, :], in_=out3[tau % 2][:]).then_inc(s, 16))
            out3_rd[tau % 2] = sout.v
        flush_block()

    wait("gpsimd", sout, sout.v)
    flush_block()

    es3.close()
    pes3.close()
    es.close()
    return nc


_NC_CACHE = None


def kernel(L0_attn_x, L0_attn_wq, L0_attn_wk, L0_attn_wv, L0_attn_wo,
           L0_ffn_gate_w, L0_ffn_gate_scale, L0_ffn_up_w, L0_ffn_up_scale,
           L0_ffn_down_w, L0_ffn_down_scale):
    global _NC_CACHE
    x = np.ascontiguousarray(np.asarray(L0_attn_x, dtype=np.float32))
    wq = np.ascontiguousarray(np.asarray(L0_attn_wq, dtype=np.float32))
    wk = np.ascontiguousarray(np.asarray(L0_attn_wk, dtype=np.float32))
    wv = np.ascontiguousarray(np.asarray(L0_attn_wv, dtype=np.float32))
    wo = np.ascontiguousarray(np.asarray(L0_attn_wo, dtype=np.float32))
    gw = np.ascontiguousarray(np.asarray(L0_ffn_gate_w, dtype=np.int32))
    uw = np.ascontiguousarray(np.asarray(L0_ffn_up_w, dtype=np.int32))
    dw = np.ascontiguousarray(np.asarray(L0_ffn_down_w, dtype=np.int32))
    gs = np.ascontiguousarray(np.asarray(L0_ffn_gate_scale, dtype=np.float32))
    us = np.ascontiguousarray(np.asarray(L0_ffn_up_scale, dtype=np.float32))
    ds = np.ascontiguousarray(np.asarray(L0_ffn_down_scale, dtype=np.float32))
    if _NC_CACHE is None:
        _NC_CACHE = build_nc()
    nc = _NC_CACHE
    in_maps = []
    for b in range(8):
        in_maps.append(dict(x=x[b], wq=wq, wk=wk, wv=wv, wo=wo,
                            gw=gw, uw=uw, dw=dw, gs=gs, us=us, ds=ds))
    res = run_bass_kernel_spmd(nc, in_maps, core_ids=list(range(8)))
    out = np.stack([np.asarray(r["out"], dtype=np.float32) for r in res.results], axis=0)
    return out


# version 17 (final)
# speedup vs baseline: 1.1246x; 1.1246x over previous
import numpy as np
import ml_dtypes
from contextlib import ExitStack

import concourse.bass as bass
import concourse.mybir as mybir
from concourse.bass_utils import run_bass_kernel_spmd

F32 = mybir.dt.float32
BF16 = mybir.dt.bfloat16
I32 = mybir.dt.int32
AF = mybir.ActivationFunctionType
ALU = mybir.AluOpType
bfdt = ml_dtypes.bfloat16

D = 2048
T = 2048
I = 8192
KC = D // 128
NTB = 4
TB = 512
NPAIR = 16


def _consts():
    inv = 1.0 / 10000.0 ** (np.arange(0, 64, 2, dtype=np.float64) / 64.0)
    t = np.arange(T, dtype=np.float64)
    f = np.outer(t, inv)
    COS = np.tile(np.cos(f).T, (4, 1)).astype(np.float32).astype(bfdt)
    sgn = np.where((np.arange(128) // 32) % 2 == 0, -1.0, 1.0)[:, None]
    SINS = (np.tile(np.sin(f).T, (4, 1)) * sgn).astype(np.float32).astype(bfdt)
    m = np.arange(128)
    nb = (m // 64) * 64 + 2 * (m % 32) + ((m % 64) // 32)
    sigma = np.where(m % 64 < 32, m + 32, m - 32)
    PNB = np.zeros((128, 128), dtype=bfdt)
    PNB[nb, m] = 1
    PCMB = np.zeros((128, 128), dtype=bfdt)
    PCMB[nb[sigma], m] = 1
    DUPLO = np.zeros((128, 128), dtype=bfdt)
    DUPLO[m % 64, m] = 1
    DUPHI = np.zeros((128, 128), dtype=bfdt)
    DUPHI[64 + (m % 64), m] = 1
    IDN = np.zeros((128, 128), dtype=bfdt)
    IDN[np.arange(128), np.arange(128)] = 1
    NIDN = np.zeros((128, 128), dtype=bfdt)
    NIDN[np.arange(128), np.arange(128)] = -1.0e5
    MASKS = []
    tk = np.arange(128)[:, None]
    tq = np.arange(512)[None, :]
    for v in range(4):
        MASKS.append((tk > tq - 128 * v).astype(np.float32).astype(bfdt))
    return COS, SINS, PNB, PCMB, DUPLO, DUPHI, IDN, NIDN, MASKS


def build_nc():
    nc = bass.Bass("TRN2", target_bir_lowering=False)
    x_h = nc.dram_tensor("x", [T, D], F32, kind="ExternalInput").ap()
    wq_h = nc.dram_tensor("wq", [D, D], F32, kind="ExternalInput").ap()
    wk_h = nc.dram_tensor("wk", [512, D], F32, kind="ExternalInput").ap()
    wv_h = nc.dram_tensor("wv", [512, D], F32, kind="ExternalInput").ap()
    wo_h = nc.dram_tensor("wo", [D, D], F32, kind="ExternalInput").ap()
    gw_h = nc.dram_tensor("gw", [I, D], I32, kind="ExternalInput").ap()
    uw_h = nc.dram_tensor("uw", [I, D], I32, kind="ExternalInput").ap()
    dw_h = nc.dram_tensor("dw", [D, I], I32, kind="ExternalInput").ap()
    gs_h = nc.dram_tensor("gs", [I], F32, kind="ExternalInput").ap()
    us_h = nc.dram_tensor("us", [I], F32, kind="ExternalInput").ap()
    ds_h = nc.dram_tensor("ds", [D], F32, kind="ExternalInput").ap()
    out_h = nc.dram_tensor("out", [T, D], F32, kind="ExternalOutput").ap()
    scr_wq = nc.dram_tensor("scr_wq", [16, 128, 2048], BF16).ap()
    scr_wk = nc.dram_tensor("scr_wk", [4, 128, 2048], BF16).ap()
    scr_wv = nc.dram_tensor("scr_wv", [128, 16, 512], BF16).ap()
    scr_wo = nc.dram_tensor("scr_wo", [16, 128, 2048], BF16).ap()
    scr_g = nc.dram_tensor("scr_g", [64, 128, 2048], BF16).ap()
    scr_u = nc.dram_tensor("scr_u", [64, 128, 2048], BF16).ap()
    scr_d = nc.dram_tensor("scr_d", [16, 128, 8192], BF16).ap()
    scr_xn = nc.dram_tensor("scr_xn", [T, D], BF16).ap()
    scr_at = nc.dram_tensor("scr_at", [D, T], BF16).ap()
    scr_ffn = nc.dram_tensor("scr_ffn", [D, T], BF16).ap()
    COS_c, SINS_c, PNB_c, PCMB_c, DLO_c, DHI_c, IDN_c, NIDN_c, MASKS_c = _consts()
    cos_d = nc.inline_tensor(COS_c, name="cos_c").ap()
    sin_d = nc.inline_tensor(SINS_c, name="sin_c").ap()
    pnb_d = nc.inline_tensor(PNB_c, name="pnb_c").ap()
    pcmb_d = nc.inline_tensor(PCMB_c, name="pcmb_c").ap()
    dlo_d = nc.inline_tensor(DLO_c, name="dlo_c").ap()
    dhi_d = nc.inline_tensor(DHI_c, name="dhi_c").ap()
    idn_d = nc.inline_tensor(IDN_c, name="idn_c").ap()
    nidn_d = nc.inline_tensor(NIDN_c, name="nidn_c").ap()
    msk_d = [nc.inline_tensor(MASKS_c[v], name=f"msk{v}_c").ap() for v in range(4)]

    es = ExitStack()

    class S:
        def __init__(self, name):
            self.h = es.enter_context(nc.semaphore(name))
            self.v = 0
    sl = S("sl")
    ss = S("ss")
    sg = S("sg")
    spe = S("spe")
    sac = S("sac")
    sdv = S("sdv")
    spl = S("spl")
    sout = S("sout")
    sxn = S("sxn")
    sv = S("sv")

    QSEM = {"sync": sl, "scalar": ss, "gpsimd": sg, "vector": sv}
    ESEM = {"tensor": spe, "scalar": sac, "vector": sdv, "gpsimd": spl}

    ENGS = ["sync", "scalar", "vector", "tensor", "gpsimd"]
    queues = {e: [] for e in ENGS}

    def flush_block():
        with nc.Block() as blk:
            for name in ENGS:
                fns = queues[name]
                if fns:
                    getattr(blk, name)(lambda e, fns=fns: [f(e) for f in fns])
        for name in ENGS:
            queues[name] = []

    def emit(eng, f):
        queues[eng].append(f)

    def wait(eng, sem, v=None):
        vv = sem.v if v is None else v
        if vv <= 0:
            return
        emit(eng, lambda e, s=sem.h, v=vv: e.wait_ge(s, v))

    def dma(eng, out, in_, **kw):
        sem = QSEM[eng]
        sem.v += 16
        emit(eng, lambda e, o=out, i=in_, k=kw, s=sem.h: e.dma_start(out=o, in_=i, **k).then_inc(s, 16))
        return sem.v

    def inc(eng, call):
        sem = ESEM[eng]
        sem.v += 1
        emit(eng, lambda e, c=call, s=sem.h: c(e).then_inc(s, 1))
        return sem.v

    def op(eng, call):
        emit(eng, call)

    def drain_dmas():
        wait("sync", sl, sl.v)
        wait("scalar", ss, ss.v)
        wait("scalar", sxn, sxn.v)
        wait("gpsimd", sg, sg.v)

    def sb_in(stack, name, shape, dt):
        return stack.enter_context(nc.sbuf_tensor(name, shape, dt))

    COS = sb_in(es, "COS", [128, T], BF16)
    SINS = sb_in(es, "SINS", [128, T], BF16)
    PNB = sb_in(es, "PNB", [128, 128], BF16)
    PCMB = sb_in(es, "PCMB", [128, 128], BF16)
    DLO = sb_in(es, "DLO", [128, 128], BF16)
    DHI = sb_in(es, "DHI", [128, 128], BF16)
    IDN = sb_in(es, "IDN", [128, 128], BF16)
    MSK = [sb_in(es, f"MSK{v}", [128, 512], BF16) for v in range(4)]
    NIDN = sb_in(es, "NIDN", [128, 128], BF16)
    gs_sb = sb_in(es, "gs_sb", [128, 64], F32)
    us_sb = sb_in(es, "us_sb", [128, 64], F32)
    ds_sb = sb_in(es, "ds_sb", [128, 16], F32)
    ones64 = sb_in(es, "ones64", [1, 64], BF16)
    epsv = sb_in(es, "epsv", [128, 1], F32)
    n128v = sb_in(es, "n128v", [128, 1], F32)

    op("gpsimd", lambda e: e.memset(ones64[:], 1.0))
    op("gpsimd", lambda e: e.memset(epsv[:], 1e-5))
    op("gpsimd", lambda e: e.memset(n128v[:], -128.0))
    dma("sync", COS[:], cos_d)
    dma("sync", SINS[:], sin_d)
    dma("sync", PNB[:], pnb_d)
    dma("sync", PCMB[:], pcmb_d)
    dma("sync", DLO[:], dlo_d)
    dma("sync", DHI[:], dhi_d)
    dma("sync", IDN[:], idn_d)
    dma("sync", NIDN[:], nidn_d)
    for v in range(4):
        dma("sync", MSK[v][:], msk_d[v])
    for c in range(64):
        dma("sync", gs_sb[:, c:c + 1], gs_h[c * 128:(c + 1) * 128][:, None])
        dma("sync", us_sb[:, c:c + 1], us_h[c * 128:(c + 1) * 128][:, None])
    for c in range(16):
        dma("sync", ds_sb[:, c:c + 1], ds_h[c * 128:(c + 1) * 128][:, None])
    wait("tensor", sl, sl.v)

    es1 = ExitStack()
    q_sb = sb_in(es1, "q_sb", [128, 16, T], BF16)
    kd_sb = sb_in(es1, "kd_sb", [128, 8, T], BF16)
    v_sb = sb_in(es1, "v_sb", [128, 16, 520], BF16)

    esp = ExitStack()
    stg32 = [sb_in(esp, f"stg32_{i}", [128, 2048], I32) for i in range(4)]
    stg16 = [sb_in(esp, f"stg16_{i}", [128, 2048], BF16) for i in range(3)]
    stgT = [sb_in(esp, f"stgT{i}", [128, 2048], BF16) for i in range(2)]

    pesp = ExitStack()
    pst = [pesp.enter_context(nc.psum_tensor(f"pst{i}", [128, 512], BF16)) for i in range(4)]

    prep = {
        "cast_done": [0, 0, 0, 0],
        "tp_done": [0, 0, 0],
        "evac_done": [("vector", 0)] * 4,
        "stgT_free": [0, 0],
        "n": 0,
        "ld": 0,
    }
    ld_vals = []

    def make_tasks(src_h, n_chunks, W, dst_fn, sub128, src_dt, done_arr, tag):
        ts = []
        ns = W // 2048
        for c in range(n_chunks):
            for s in range(ns):
                ts.append(dict(src=src_h[c * 128:(c + 1) * 128, s * 2048:(s + 1) * 2048],
                               dst=dst_fn(c, s), sub=sub128, dt=src_dt,
                               done=done_arr, c=c, last=(s == ns - 1), tag=tag))
        return ts

    qdone = [0] * 16
    kdone = [0] * 4
    vdone = [0]
    odone = [0] * 16
    gdone = [0] * 64
    udone = [0] * 64
    ddone = [0] * 16

    tasks_attn = (
        make_tasks(wq_h, 16, D, lambda c, s: scr_wq[c][:, :], False, F32, qdone, "q")
        + make_tasks(wk_h, 4, D, lambda c, s: scr_wk[c][:, :], False, F32, kdone, "k")
        + make_tasks(wv_h, 4, D, lambda c, s: scr_wv[:, :, c * 128:(c + 1) * 128], False, F32, vdone, "v")
    )
    tasks_rest = make_tasks(wo_h, 16, D, lambda c, s: scr_wo[c][:, :], False, F32, odone, "o")
    tg = make_tasks(gw_h, 64, D, lambda c, s: scr_g[c][:, :], True, I32, gdone, "g")
    tu = make_tasks(uw_h, 64, D, lambda c, s: scr_u[c][:, :], True, I32, udone, "u")
    for a, b in zip(tg, tu):
        tasks_rest.append(a)
        tasks_rest.append(b)
    tasks_rest += make_tasks(dw_h, 16, I, lambda c, s: scr_d[c][:, s * 2048:(s + 1) * 2048], True, I32, ddone, "d")

    tasks = tasks_attn + tasks_rest
    pending = tasks

    EVAC = {"vector": sdv, "scalar": sac}

    def emit_load(idx):
        t = tasks[idx]
        i = idx % 4
        ldq = "gpsimd" if idx % 2 == 0 else "scalar"
        wait(ldq, sac, prep["cast_done"][i])
        ld = dma(ldq, stg32[i][:, :] if t["dt"] == I32 else
                 stg32[i].ap().bitcast(F32)[:, 0:2048], t["src"])
        ld_vals.append((QSEM[ldq], ld))

    def pump(n):
        for _ in range(n):
            k = prep["n"]
            if k >= len(tasks):
                return
            while prep["ld"] < min(len(tasks), k + 4):
                emit_load(prep["ld"])
                prep["ld"] += 1
            t = tasks[k]
            prep["n"] += 1
            i = k % 4
            i16 = k % 3
            j = k % 2
            ldsem, ld = ld_vals[k]
            wait("scalar", ldsem, ld)
            wait("scalar", spe, prep["tp_done"][i16])
            if t["sub"]:
                cd = inc("scalar", lambda e, i=i, i16=i16: e.activation(
                    out=stg16[i16][:, :], in_=stg32[i][:, 0:2048], func=AF.Identity, bias=n128v[:]))
            else:
                cd = inc("scalar", lambda e, i=i, i16=i16: e.activation(
                    out=stg16[i16][:, :], in_=stg32[i].ap().bitcast(F32)[:, 0:2048], func=AF.Copy))
            prep["cast_done"][i] = cd
            wait("tensor", sac, cd)
            ev_vals = {"vector": 0, "scalar": 0}
            gated = {"vector": False, "scalar": False}
            for tt in range(4):
                (pe_eng, pe_val) = prep["evac_done"][tt]
                wait("tensor", EVAC[pe_eng], pe_val)
                tpl = 0
                for q in range(4):
                    kc = 4 * tt + q
                    tpl = inc("tensor", lambda e, tt=tt, q=q, kc=kc, i16=i16: e.transpose(
                        out=pst[tt][:, q * 128:(q + 1) * 128],
                        in_=stg16[i16][:, kc * 128:(kc + 1) * 128], identity=IDN[:]))
                if tt == 3:
                    prep["tp_done"][i16] = tpl
                eng = "vector" if (k * 4 + tt) % 2 == 0 else "scalar"
                if not gated[eng]:
                    wait(eng, sl, prep["stgT_free"][j])
                    gated[eng] = True
                wait(eng, spe, tpl)
                if eng == "vector":
                    ev = inc(eng, lambda e, tt=tt, j=j: e.tensor_copy(
                        out=stgT[j][:, tt * 512:(tt + 1) * 512], in_=pst[tt][:, 0:512]))
                else:
                    ev = inc(eng, lambda e, tt=tt, j=j: e.activation(
                        out=stgT[j][:, tt * 512:(tt + 1) * 512], in_=pst[tt][:, 0:512], func=AF.Copy))
                prep["evac_done"][tt] = (eng, ev)
                ev_vals[eng] = ev
                first_evac = False
            wait("sync", sdv, ev_vals["vector"])
            wait("sync", sac, ev_vals["scalar"])
            dst = t["dst"]
            if t["tag"] == "v":
                st = dma("sync", dst, stgT[j][:, 0:2048].rearrange("p (a b) -> p a b", a=16))
            else:
                st = dma("sync", dst, stgT[j][:, 0:2048])
            prep["stgT_free"][j] = st
            if t["tag"] == "v":
                vdone[0] = st
            elif t["last"]:
                t["done"][t["c"]] = st

    pump(len(tasks))
    drain_dmas()
    flush_block()
    pesp.close()
    esp.close()

    es1b = ExitStack()
    xt_s = sb_in(es1b, "xt0", [128, D], F32)
    xt = [xt_s, xt_s]
    xn_tm = sb_in(es1b, "xn_tm", [128, D], BF16)
    xn_fm = sb_in(es1b, "xn_fm", [128, KC, TB], BF16)
    sumv = sb_in(es1b, "sumv", [128, 1], F32)
    sqv = sb_in(es1b, "sqv", [128, 1], F32)
    mv = sb_in(es1b, "mv", [128, 1], F32)
    ex2v = sb_in(es1b, "ex2v", [128, 1], F32)
    varv = sb_in(es1b, "varv", [128, 1], F32)
    rstdv = sb_in(es1b, "rstdv", [128, 1], F32)
    nmrv = sb_in(es1b, "nmrv", [128, 1], F32)
    wbuf = [sb_in(es1b, f"wbuf{i}", [128, 2048], BF16) for i in range(2)]
    wvb = sb_in(es1b, "wvb", [128, 16, 512], BF16)
    qtmp = [sb_in(es1b, f"qtmp{i}", [128, 512], BF16) for i in range(2)]
    t1b = sb_in(es1b, "t1b", [128, 512], BF16)
    t2b = sb_in(es1b, "t2b", [128, 512], BF16)
    krope = sb_in(es1b, "krope", [128, 4, 512], BF16)

    pes = ExitStack()
    pp = [pes.enter_context(nc.psum_tensor(f"pp{i}", [128, 512], F32)) for i in range(2)]
    pa_ps = pes.enter_context(nc.psum_tensor("pa_ps", [128, 512], F32))
    pb_ps = pes.enter_context(nc.psum_tensor("pb_ps", [128, 512], F32))

    for tt in range(16):
        for j in range(8):
            op("vector", lambda e, tt=tt, j=j: e.memset(v_sb[:, tt, j * 65 + 64:j * 65 + 65], 1.0))

    xt_rd = [0]
    xntm_rd = [0]
    sqv_rd = [0]
    qtmp_st = [(0, 0), (0, 0)]
    pp_rd = [0, 0]
    pa_rd = [0]
    pb_rd = [0]
    rope_flip = [0]

    def ln_tile(b, tau):
        t0 = (4 * b + tau) * 128
        xe = xt[tau % 2]
        wait("sync", sac, xt_rd[0])
        lv = dma("sync", xe[:], x_h[t0:t0 + 128, :])
        wait("vector", sl, lv)
        sdv_red = inc("vector", lambda e, xe=xe: e.reduce_sum(sumv[:], xe[:], axis=mybir.AxisListType.X))
        wait("scalar", sl, lv)
        wait("scalar", sdv, sqv_rd[0])
        wait("scalar", sxn, xntm_rd[0])
        sac_sq = inc("scalar", lambda e, xe=xe: e.activation(
            out=xn_tm[:], in_=xe[:], func=AF.Square, accum_out=sqv[:]))
        wait("vector", sdv, sdv_red)
        sdv_mv = inc("vector", lambda e: e.tensor_scalar_mul(out=mv[:], in0=sumv[:], scalar1=1.0 / D))
        wait("vector", sac, sac_sq)
        sqv_rd[0] = inc("vector", lambda e: e.tensor_scalar_mul(out=ex2v[:], in0=sqv[:], scalar1=1.0 / D))
        wait("vector", sdv, sdv_mv)
        sdv_vm = inc("vector", lambda e: e.tensor_tensor(out=varv[:], in0=mv[:], in1=mv[:], op=ALU.mult))
        wait("vector", sdv, sdv_vm)
        sdv_var = inc("vector", lambda e: e.tensor_tensor(out=varv[:], in0=ex2v[:], in1=varv[:], op=ALU.subtract))
        wait("scalar", sdv, sdv_var)
        sac_ln1 = inc("scalar", lambda e: e.activation(out=rstdv[:], in_=varv[:], func=AF.Ln, bias=epsv[:]))
        wait("scalar", sac, sac_ln1)
        sac_r = inc("scalar", lambda e: e.activation(out=rstdv[:], in_=rstdv[:], func=AF.Exp, scale=-0.5))
        wait("vector", sac, sac_r)
        sdv_nmr = inc("vector", lambda e: e.tensor_tensor(out=nmrv[:], in0=mv[:], in1=rstdv[:], op=ALU.mult))
        wait("vector", sdv, sdv_nmr)
        sdv_nmr2 = inc("vector", lambda e: e.tensor_scalar_mul(out=nmrv[:], in0=nmrv[:], scalar1=-1.0))
        wait("scalar", sdv, sdv_nmr2)
        sac_xn = inc("scalar", lambda e, xe=xe: e.activation(
            out=xn_tm[:], in_=xe[:], func=AF.Identity, scale=rstdv[:], bias=nmrv[:]))
        xt_rd[0] = sac_xn
        wait("scalar", sac, sac_xn)
        sxn.v += 16
        emit("scalar", lambda e, t0=t0, s=sxn.h: e.dma_start(
            out=scr_xn[t0:t0 + 128, :], in_=xn_tm[:]).then_inc(s, 16))
        xntm_rd[0] = sxn.v

    def rope_chunk(b, ps_src, ps_idx, dst_ap):
        fl = rope_flip[0]
        rope_flip[0] ^= 1
        qt = qtmp[fl]
        (g_spe, g_sdv) = qtmp_st[fl]
        wait("scalar", spe, g_spe)
        wait("scalar", sdv, g_sdv)
        sac_ev = inc("scalar", lambda e, qt=qt, p=ps_src: e.activation(out=qt[:], in_=p[:], func=AF.Copy))
        if ps_idx is not None:
            pp_rd[ps_idx] = sac_ev
        wait("tensor", sac, sac_ev)
        wait("tensor", sdv, pa_rd[0])
        spe_a = inc("tensor", lambda e, qt=qt: e.matmul(out=pa_ps[:], lhsT=PNB[:], rhs=qt[:],
                                                        start=True, stop=True))
        wait("tensor", sdv, pb_rd[0])
        spe_b = inc("tensor", lambda e, qt=qt: e.matmul(out=pb_ps[:], lhsT=PCMB[:], rhs=qt[:],
                                                        start=True, stop=True))
        bsl = slice(b * TB, (b + 1) * TB)
        wait("vector", spe, spe_a)
        sdv_t1 = inc("vector", lambda e, bsl=bsl: e.tensor_tensor(
            out=t1b[:], in0=pa_ps[:], in1=COS[:, bsl], op=ALU.mult))
        pa_rd[0] = sdv_t1
        wait("vector", spe, spe_b)
        sdv_t2 = inc("vector", lambda e, bsl=bsl: e.tensor_tensor(
            out=t2b[:], in0=pb_ps[:], in1=SINS[:, bsl], op=ALU.mult))
        pb_rd[0] = sdv_t2
        wait("vector", sdv, sdv_t2)
        sdv_o = inc("vector", lambda e, d=dst_ap: e.tensor_tensor(out=d, in0=t1b[:], in1=t2b[:], op=ALU.add))
        qtmp_st[fl] = (spe_b, sdv_t1)
        return sac_ev, sdv_o

    wbuf_rd = [0, 0]
    wvb_rd = [0]
    v_ev = [0]

    for b in range(NTB):
        bsl = slice(b * TB, (b + 1) * TB)
        for tau in range(4):
            ln_tile(b, tau)
        wait("sync", sxn, sxn.v)
        for c in range(KC):
            dma("sync", xn_fm[:, c, :], scr_xn[bsl, c * 128:(c + 1) * 128], transpose=True)
        xn_ready = sl.v
        for mu in range(16):
            wb = wbuf[mu % 2]
            wait("sync", spe, wbuf_rd[mu % 2])
            wait("sync", sl, qdone[mu])
            dma("sync", wb[:, :], scr_wq[mu])
            ld_done = sl.v
            pe = pp[mu % 2]
            wait("tensor", sl, ld_done)
            wait("tensor", sl, xn_ready)
            wait("tensor", sac, pp_rd[mu % 2])
            last = 0
            for k in range(KC):
                last = inc("tensor", lambda e, k=k, pe=pe, wb=wb: e.matmul(
                    out=pe[:], lhsT=wb[:, k * 128:(k + 1) * 128], rhs=xn_fm[:, k, :],
                    start=(k == 0), stop=(k == KC - 1)))
            wbuf_rd[mu % 2] = last
            wait("scalar", spe, last)
            rope_chunk(b, pe, mu % 2, q_sb[:, mu, bsl])
        for mu in range(4):
            wbk = wbuf[mu % 2]
            wait("sync", spe, wbuf_rd[mu % 2])
            wait("sync", sl, kdone[mu])
            dma("sync", wbk[:, :], scr_wk[mu])
            ld_done = sl.v
            pe = pp[mu % 2]
            wait("tensor", sl, ld_done)
            wait("tensor", sl, xn_ready)
            wait("tensor", sac, pp_rd[mu % 2])
            last = 0
            for k in range(KC):
                last = inc("tensor", lambda e, k=k, pe=pe, wbk=wbk: e.matmul(
                    out=pe[:], lhsT=wbk[:, k * 128:(k + 1) * 128], rhs=xn_fm[:, k, :],
                    start=(k == 0), stop=(k == KC - 1)))
            wbuf_rd[mu % 2] = last
            wait("scalar", spe, last)
            _, sdv_o = rope_chunk(b, pe, mu % 2, krope[:, mu, :])
            wait("tensor", sdv, sdv_o)
            for half, PM in ((0, DLO), (1, DHI)):
                wait("tensor", sdv, pa_rd[0])
                spe_d = inc("tensor", lambda e, PM=PM, mu=mu: e.matmul(
                    out=pa_ps[:], lhsT=PM[:], rhs=krope[:, mu, :], start=True, stop=True))
                wait("scalar", spe, spe_d)
                sac_e = inc("scalar", lambda e, mu=mu, half=half, bsl=bsl: e.activation(
                    out=kd_sb[:, 2 * mu + half, bsl], in_=pa_ps[:], func=AF.Copy))
                pa_rd[0] = 0
                wait("tensor", sac, sac_e)
        wait("sync", spe, wvb_rd[0])
        wait("sync", sl, vdone[0])
        dma("sync", wvb[:, :, :], scr_wv)
        ld_done = sl.v
        for tau in range(4):
            pe0 = pp[tau % 2]
            wait("tensor", sl, ld_done)
            wait("tensor", sl, xn_ready)
            wait("tensor", sac, pp_rd[tau % 2])
            wait("tensor", sdv, v_ev[0])
            last = 0
            for k in range(KC):
                last = inc("tensor", lambda e, k=k, pe0=pe0, tau=tau: e.matmul(
                    out=pe0[:], lhsT=xn_fm[:, k, tau * 128:(tau + 1) * 128],
                    rhs=wvb[:, k, :], start=(k == 0), stop=(k == KC - 1)))
            wvb_rd[0] = last
            wait("vector", spe, last)
            vout = v_sb[:, 4 * b + tau, 0:520].rearrange("p (h x) -> p h x", h=8)[:, :, 0:64]
            vin = pe0[:, 0:512].rearrange("p (h x) -> p h x", h=8)
            sdv_v = inc("vector", lambda e, vout=vout, vin=vin: e.tensor_copy(out=vout, in_=vin))
            v_ev[0] = sdv_v
            wait("tensor", sdv, sdv_v)
            pp_rd[tau % 2] = 0
        if b == NTB - 1:
            drain_dmas()
        flush_block()

    es1b.close()
    pes.close()

    es2 = ExitStack()
    wob = [sb_in(es2, f"wob{i}", [128, 2048], BF16) for i in range(2)]
    pbuf = [sb_in(es2, f"pbuf{i}", [128, 1024], BF16) for i in range(2)]
    afm = sb_in(es2, "afm", [128, 16, 512], BF16)
    lnv2 = sb_in(es2, "lnv2", [1, 512], F32)
    rbf = sb_in(es2, "rbf", [1, 512], BF16)
    rb64 = sb_in(es2, "rb64", [64, 512], BF16)
    tmp64 = sb_in(es2, "tmp64", [64, 512], BF16)
    woev = [sb_in(es2, f"woev{i}", [128, 512], BF16) for i in range(2)]

    pes2 = ExitStack()
    psc = [pes2.enter_context(nc.psum_tensor(f"psc{i}", [128, 1024], F32)) for i in range(2)]
    pav = [pes2.enter_context(nc.psum_tensor(f"pav{i}", [128, 512], F32)) for i in range(4)]

    norm_state = {"rbf_rd": 0, "rb64_rd": 0}

    for B in range(NTB):
        Bsl = slice(B * TB, (B + 1) * TB)
        nck = 4 * B + 4
        for s in range(2):
            wait("sync", sl, odone[s])
            dma("sync", wob[s][:, :], scr_wo[s])
        wo_ld01 = sl.v

        exp_read = {}
        av_tt = {}
        av_last = {}
        norm_q = []

        def do_norm(e_pair, pidx):
            for hh in range(2):
                pa = pav[pidx + hh]
                sac_ln = inc("scalar", lambda e, pa=pa: e.activation(
                    out=lnv2[:], in_=pa[64:65, :], func=AF.Ln))
                wait("scalar", spe, norm_state["rbf_rd"])
                wait("scalar", sac, sac_ln)
                sac_r = inc("scalar", lambda e: e.activation(
                    out=rbf[:], in_=lnv2[:], func=AF.Exp, scale=-1.0))
                wait("tensor", sac, sac_r)
                spe_b = inc("tensor", lambda e, pa=pa: e.matmul(
                    out=pa[64:128, :], lhsT=ones64[:], rhs=rbf[:], start=True, stop=True))
                norm_state["rbf_rd"] = spe_b
                wait("scalar", spe, spe_b)
                wait("scalar", sdv, norm_state["rb64_rd"])
                sac_rb = inc("scalar", lambda e, pa=pa: e.activation(
                    out=rb64[:], in_=pa[64:128, :], func=AF.Copy))
                wait("vector", sac, sac_rb)
                if hh == 0:
                    sdv_tt = inc("vector", lambda e, pa=pa, ep=e_pair: e.tensor_tensor(
                        out=afm[0:64, ep, :], in0=pa[0:64, :], in1=rb64[:], op=ALU.mult))
                else:
                    sdv_t64 = inc("vector", lambda e, pa=pa: e.tensor_tensor(
                        out=tmp64[:], in0=pa[0:64, :], in1=rb64[:], op=ALU.mult))
                    wait("vector", sdv, sdv_t64)
                    sdv_tt = inc("vector", lambda e, ep=e_pair: e.tensor_scalar_mul(
                        out=afm[64:128, ep, :], in0=tmp64[:], scalar1=1.0))
                norm_state["rb64_rd"] = sdv_tt
                av_tt[pidx + hh] = sdv_tt

        for e_pair in range(NPAIR):
            j = e_pair // 2
            pidx = (e_pair % 2) * 2
            for hh in range(2):
                if (pidx + hh) in av_tt:
                    wait("tensor", sdv, av_tt[pidx + hh])
            av_q = []
            for c in range(nck):
                ps = psc[c % 2]
                if (c % 2) in exp_read:
                    wait("tensor", sac, exp_read[c % 2])
                inc("tensor", lambda e, ps=ps, j=j, c=c, ep=e_pair, Bsl=Bsl: e.matmul(
                    out=ps[:, 0:512], lhsT=kd_sb[0:64, j, c * 128:(c + 1) * 128],
                    rhs=q_sb[0:64, ep, Bsl], start=True, stop=(c < 4 * B)))
                spe_s = inc("tensor", lambda e, ps=ps, j=j, c=c, ep=e_pair, Bsl=Bsl: e.matmul(
                    out=ps[:, 512:1024], lhsT=kd_sb[64:128, j, c * 128:(c + 1) * 128],
                    rhs=q_sb[64:128, ep, Bsl], start=True, stop=(c < 4 * B)))
                if c >= 4 * B:
                    vmk = MSK[c - 4 * B]
                    inc("tensor", lambda e, ps=ps, vmk=vmk: e.matmul(
                        out=ps[:, 0:512], lhsT=NIDN[:], rhs=vmk[:], start=False, stop=True))
                    spe_s = inc("tensor", lambda e, ps=ps, vmk=vmk: e.matmul(
                        out=ps[:, 512:1024], lhsT=NIDN[:], rhs=vmk[:], start=False, stop=True))
                if av_q:
                    av_q.pop(0)()
                wait("scalar", spe, spe_s)
                if (c % 2) in av_last:
                    wait("scalar", spe, av_last[c % 2])
                sac_e = inc("scalar", lambda e, ps=ps, c=c: e.activation(
                    out=pbuf[c % 2][:], in_=ps[:], func=AF.Exp, scale=0.125))
                exp_read[c % 2] = sac_e

                def do_av(c=c, j=j, pidx=pidx, nck=nck, sac_e=sac_e):
                    wait("tensor", sac, sac_e)
                    inc("tensor", lambda e, c=c, j=j, pidx=pidx, nck=nck: e.matmul(
                        out=pav[pidx][0:65, :], lhsT=v_sb[:, c, j * 65:(j + 1) * 65],
                        rhs=pbuf[c % 2][:, 0:512], start=(c == 0), stop=(c == nck - 1)))
                    spe_av = inc("tensor", lambda e, c=c, j=j, pidx=pidx, nck=nck: e.matmul(
                        out=pav[pidx + 1][0:65, :], lhsT=v_sb[:, c, j * 65:(j + 1) * 65],
                        rhs=pbuf[c % 2][:, 512:1024], start=(c == 0), stop=(c == nck - 1)))
                    av_last[c % 2] = spe_av
                av_q.append(do_av)
            while av_q:
                av_q.pop(0)()
            if norm_q:
                norm_q.pop(0)()
            norm_q.append(lambda ep=e_pair, pidx=pidx: do_norm(ep, pidx))
        while norm_q:
            norm_q.pop(0)()
        attn_done = sdv.v
        woev_rd = [0, 0]
        wob_rd = [0, 0]
        for mu in range(16):
            wot = wob[mu % 2]
            if mu >= 2:
                wait("sync", spe, wob_rd[mu % 2])
                wait("sync", sl, odone[mu])
                dma("sync", wot[:, :], scr_wo[mu])
                ld_done = sl.v
            else:
                ld_done = wo_ld01
            pw = psc[mu % 2]
            wait("tensor", sl, ld_done)
            wait("tensor", sdv, attn_done)
            if (mu % 2) in exp_read:
                wait("tensor", sac, exp_read.pop(mu % 2))
            last = 0
            for dv in range(16):
                last = inc("tensor", lambda e, dv=dv, pw=pw, wot=wot: e.matmul(
                    out=pw[:, 0:512], lhsT=wot[:, dv * 128:(dv + 1) * 128], rhs=afm[:, dv, :],
                    start=(dv == 0), stop=(dv == 15)))
            wob_rd[mu % 2] = last
            wait("scalar", spe, last)
            wait("scalar", ss, woev_rd[mu % 2])
            sac_ev = inc("scalar", lambda e, pw=pw, mu=mu: e.activation(
                out=woev[mu % 2][:], in_=pw[:, 0:512], func=AF.Copy))
            exp_read[mu % 2] = sac_ev
            wait("scalar", sac, sac_ev)
            dma("scalar", scr_at[mu * 128:(mu + 1) * 128, Bsl], woev[mu % 2][:])
            woev_rd[mu % 2] = ss.v
        if B == NTB - 1:
            drain_dmas()
        flush_block()

    pes2.close()
    es2.close()
    es1.close()

    es3 = ExitStack()
    h_sb = sb_in(es3, "h_sb", [128, 64, 512], BF16)
    xnf = sb_in(es3, "xnf", [128, KC, 512], BF16)
    gwide = [sb_in(es3, f"gwide{i}", [128, 2, 2048], BF16) for i in range(2)]
    uwide = [sb_in(es3, f"uwide{i}", [128, 2, 2048], BF16) for i in range(2)]
    dbw = [sb_in(es3, f"dbw{i}", [128, 4096], BF16) for i in range(2)]
    sgt = [sb_in(es3, f"sgt{i}", [128, 512], BF16) for i in range(2)]
    sut = [sb_in(es3, f"sut{i}", [128, 512], BF16) for i in range(2)]
    fdev = [sb_in(es3, f"fdev{i}", [128, 512], BF16) for i in range(2)]
    xt3 = [sb_in(es3, f"xt3{i}", [128, D], F32) for i in range(2)]
    at3 = [sb_in(es3, f"at3{i}", [128, D], BF16) for i in range(2)]
    ft3 = [sb_in(es3, f"ft3{i}", [128, D], BF16) for i in range(2)]
    out3 = [sb_in(es3, f"out3{i}", [128, D], F32) for i in range(2)]

    pes3 = ExitStack()
    pg = [pes3.enter_context(nc.psum_tensor(f"pg{i}", [128, 512], F32)) for i in range(4)]
    pu = [pes3.enter_context(nc.psum_tensor(f"pu{i}", [128, 512], F32)) for i in range(4)]

    gu_mm = [0, 0]
    sgt_rd = [0, 0]
    gu_ev_sac = [0, 0]
    gu_ev_sdv = [0, 0]
    dn_mm = {}
    dn_ev = {}
    out3_rd = [0, 0]
    xt3_rd = [0, 0]
    at3_rd = [0, 0]
    fdev_rd = [0, 0]

    for b in range(NTB):
        bsl = slice(b * TB, (b + 1) * TB)
        for c in range(KC):
            dma("sync", xnf[:, c, :], scr_xn[bsl, c * 128:(c + 1) * 128], transpose=True)
        xn_ld = sl.v
        for sp in range(32):
            set_ = sp % 2
            gw_t, uw_t = gwide[set_], uwide[set_]
            wait("sync", spe, gu_mm[set_])
            if b == 0:
                wait("sync", sl, max(gdone[2 * sp + 1], udone[2 * sp + 1]))
            dma("sync", gw_t[:, :, :], scr_g[2 * sp:2 * sp + 2].rearrange("c p w -> p c w"))
            dma("sync", uw_t[:, :, :], scr_u[2 * sp:2 * sp + 2].rearrange("c p w -> p c w"))
            ld_done = sl.v
            pga, pgb_ = pg[2 * set_], pg[2 * set_ + 1]
            pua, pub = pu[2 * set_], pu[2 * set_ + 1]
            wait("tensor", sl, ld_done)
            wait("tensor", sl, xn_ld)
            wait("tensor", sac, gu_ev_sac[set_])
            wait("tensor", sdv, gu_ev_sdv[set_])
            if sp < 2 and dn_ev:
                wait("tensor", sac, max(dn_ev.values()))
            last = 0
            for k in range(KC):
                inc("tensor", lambda e, k=k, gw_t=gw_t, pga=pga: e.matmul(
                    out=pga[:], lhsT=gw_t[:, 0, k * 128:(k + 1) * 128], rhs=xnf[:, k, :],
                    start=(k == 0), stop=(k == KC - 1)))
                inc("tensor", lambda e, k=k, gw_t=gw_t, pgb_=pgb_: e.matmul(
                    out=pgb_[:], lhsT=gw_t[:, 1, k * 128:(k + 1) * 128], rhs=xnf[:, k, :],
                    start=(k == 0), stop=(k == KC - 1)))
                inc("tensor", lambda e, k=k, uw_t=uw_t, pua=pua: e.matmul(
                    out=pua[:], lhsT=uw_t[:, 0, k * 128:(k + 1) * 128], rhs=xnf[:, k, :],
                    start=(k == 0), stop=(k == KC - 1)))
                last = inc("tensor", lambda e, k=k, uw_t=uw_t, pub=pub: e.matmul(
                    out=pub[:], lhsT=uw_t[:, 1, k * 128:(k + 1) * 128], rhs=xnf[:, k, :],
                    start=(k == 0), stop=(k == KC - 1)))
            gu_mm[set_] = last
            wait("scalar", spe, last)
            wait("vector", spe, last)
            e_sac = e_sdv = 0
            for d in range(2):
                ci = sp * 2 + d
                pgx = pga if d == 0 else pgb_
                pux = pua if d == 0 else pub
                wait("scalar", sdv, sgt_rd[d])
                e_sac = inc("scalar", lambda e, pgx=pgx, ci=ci, d=d: e.activation(
                    out=sgt[d][:], in_=pgx[:], func=AF.Silu, scale=gs_sb[:, ci:ci + 1]))
                e_sdv0 = inc("vector", lambda e, pux=pux, ci=ci, d=d: e.tensor_single_scalar(
                    out=sut[d][:], in_=pux[:], scalar=us_sb[:, ci:ci + 1], op=ALU.mult))
                wait("vector", sac, e_sac)
                wait("vector", sdv, e_sdv0)
                e_sdv = inc("vector", lambda e, ci=ci, d=d: e.tensor_tensor(
                    out=h_sb[:, ci, :], in0=sgt[d][:], in1=sut[d][:], op=ALU.mult))
                sgt_rd[d] = e_sdv
            gu_ev_sac[set_] = e_sac
            gu_ev_sdv[set_] = e_sdv
        h_done = sdv.v
        gu_done_sac = sac.v
        pbank = pg + pu
        for mc in range(16):
            bi = mc % 8
            pb_ = pbank[bi]
            last = 0
            for hf in range(2):
                gbuf = dbw[hf]
                wait("sync", spe, dn_mm.get(hf, 0))
                if b == 0 and hf == 0:
                    wait("sync", sl, ddone[mc])
                dma("sync", gbuf[:, :], scr_d[mc][:, hf * 4096:(hf + 1) * 4096])
                ld_done = sl.v
                wait("tensor", sl, ld_done)
                if hf == 0:
                    if mc < 8:
                        wait("tensor", sdv, h_done)
                        wait("tensor", sac, gu_done_sac)
                    if bi in dn_ev:
                        wait("tensor", sac, dn_ev[bi])
                for q in range(32):
                    ic = hf * 32 + q
                    last = inc("tensor", lambda e, pb_=pb_, gbuf=gbuf, q=q, ic=ic: e.matmul(
                        out=pb_[:], lhsT=gbuf[:, q * 128:(q + 1) * 128],
                        rhs=h_sb[:, ic, :], start=(ic == 0), stop=(ic == 63)))
                dn_mm[hf] = last
            wait("scalar", spe, last)
            wait("scalar", ss, fdev_rd[mc % 2])
            sac_d = inc("scalar", lambda e, pb_=pb_, mc=mc: e.activation(
                out=fdev[mc % 2][:], in_=pb_[:], func=AF.Identity, scale=ds_sb[:, mc:mc + 1]))
            dn_ev[bi] = sac_d
            wait("scalar", sac, sac_d)
            dma("scalar", scr_ffn[mc * 128:(mc + 1) * 128, bsl], fdev[mc % 2][:])
            fdev_rd[mc % 2] = ss.v
        fst_done = ss.v
        wait("sync", ss, fst_done)
        for tau in range(4):
            t0 = (4 * b + tau) * 128
            wait("sync", sdv, xt3_rd[tau % 2])
            lx = dma("sync", xt3[tau % 2][:], x_h[t0:t0 + 128, :])
            wait("sync", sdv, at3_rd[tau % 2])
            la = dma("sync", at3[tau % 2][:], scr_at[:, t0:t0 + 128], transpose=True)
            lf = dma("sync", ft3[tau % 2][:], scr_ffn[:, t0:t0 + 128], transpose=True)
            wait("vector", sl, lf)
            wait("vector", sout, out3_rd[tau % 2])
            sdv_a = inc("vector", lambda e, tau=tau: e.tensor_tensor(
                out=out3[tau % 2][:], in0=xt3[tau % 2][:], in1=at3[tau % 2][:], op=ALU.add))
            xt3_rd[tau % 2] = sdv_a
            at3_rd[tau % 2] = sdv_a
            wait("vector", sdv, sdv_a)
            sdv_o = inc("vector", lambda e, tau=tau: e.tensor_tensor(
                out=out3[tau % 2][:], in0=out3[tau % 2][:], in1=ft3[tau % 2][:], op=ALU.add))
            wait("gpsimd", sdv, sdv_o)
            sout.v += 16
            emit("gpsimd", lambda e, t0=t0, tau=tau, s=sout.h: e.dma_start(
                out=out_h[t0:t0 + 128, :], in_=out3[tau % 2][:]).then_inc(s, 16))
            out3_rd[tau % 2] = sout.v
        flush_block()

    wait("gpsimd", sout, sout.v)
    flush_block()

    es3.close()
    pes3.close()
    es.close()
    return nc


_NC_CACHE = None


def kernel(L0_attn_x, L0_attn_wq, L0_attn_wk, L0_attn_wv, L0_attn_wo,
           L0_ffn_gate_w, L0_ffn_gate_scale, L0_ffn_up_w, L0_ffn_up_scale,
           L0_ffn_down_w, L0_ffn_down_scale):
    global _NC_CACHE
    x = np.ascontiguousarray(np.asarray(L0_attn_x, dtype=np.float32))
    wq = np.ascontiguousarray(np.asarray(L0_attn_wq, dtype=np.float32))
    wk = np.ascontiguousarray(np.asarray(L0_attn_wk, dtype=np.float32))
    wv = np.ascontiguousarray(np.asarray(L0_attn_wv, dtype=np.float32))
    wo = np.ascontiguousarray(np.asarray(L0_attn_wo, dtype=np.float32))
    gw = np.ascontiguousarray(np.asarray(L0_ffn_gate_w, dtype=np.int32))
    uw = np.ascontiguousarray(np.asarray(L0_ffn_up_w, dtype=np.int32))
    dw = np.ascontiguousarray(np.asarray(L0_ffn_down_w, dtype=np.int32))
    gs = np.ascontiguousarray(np.asarray(L0_ffn_gate_scale, dtype=np.float32))
    us = np.ascontiguousarray(np.asarray(L0_ffn_up_scale, dtype=np.float32))
    ds = np.ascontiguousarray(np.asarray(L0_ffn_down_scale, dtype=np.float32))
    if _NC_CACHE is None:
        _NC_CACHE = build_nc()
    nc = _NC_CACHE
    in_maps = []
    for b in range(8):
        in_maps.append(dict(x=x[b], wq=wq, wk=wk, wv=wv, wo=wo,
                            gw=gw, uw=uw, dw=dw, gs=gs, us=us, ds=ds))
    res = run_bass_kernel_spmd(nc, in_maps, core_ids=list(range(8)))
    out = np.stack([np.asarray(r["out"], dtype=np.float32) for r in res.results], axis=0)
    return out
